# version 01
# baseline (speedup 1.0000x reference)
import contextlib
import numpy as np
import ml_dtypes
import concourse.bass as bass
import concourse.mybir as mybir
from concourse.bass_utils import run_bass_kernel_spmd

F32 = mybir.dt.float32
BF16 = mybir.dt.bfloat16
I32 = mybir.dt.int32
AF = mybir.ActivationFunctionType
ALU = mybir.AluOpType
AXX = mybir.AxisListType.X
BFD = ml_dtypes.bfloat16

NEG = -1.0e30
ARO = [0, 4, 1, 5, 2, 6, 3, 7]
INV_SQRT_512 = 1.0 / float(np.sqrt(512.0))
INV_SQRT_128 = 1.0 / float(np.sqrt(128.0))


class Prog:

    def __init__(self):
        self.ops = []

    def dma(self, out, in_, **kw):
        self.ops.append(
            ("sync", (lambda e, o=out, i=in_, k=dict(kw): e.dma_start(out=o, in_=i, **k)), 16)
        )

    def t(self, fn):
        self.ops.append(("tensor", fn, 1))

    def v(self, fn):
        self.ops.append(("vector", fn, 1))

    def s(self, fn):
        self.ops.append(("scalar", fn, 1))

    def finish(self, nc):
        cum = [0]
        for (_, _, inc) in self.ops:
            cum.append(cum[-1] + inc)
        total = cum[-1]
        with nc.semaphore("SER") as sem, nc.Block() as block:
            def mk(name):
                def body(eng):
                    for i, (en, fn, inc) in enumerate(self.ops):
                        if en != name:
                            continue
                        eng.wait_ge(sem, cum[i])
                        fn(eng).then_inc(sem, inc)
                    eng.wait_ge(sem, total)
                return body
            block.sync(mk("sync"))
            block.tensor(mk("tensor"))
            block.vector(mk("vector"))
            block.scalar(mk("scalar"))


def _split_dma(p, dst3, src3, nchunks, split):
    step = (nchunks + split - 1) // split
    for c0 in range(0, nchunks, step):
        c1 = min(c0 + step, nchunks)
        p.dma(dst3[:, c0:c1, :], src3[:, c0:c1, :])


def _layernorm_tail(p, z_ap, zc, tmps, out_f32_tile, out_bf_tile=None):
    sm, negmean, stdv, rstd = tmps
    p.v(lambda e, z=z_ap: e.reduce_sum(sm[:, 0:1], z, axis=AXX))
    p.v(lambda e: e.tensor_scalar_mul(negmean[:, 0:1], sm[:, 0:1], -1.0 / 4096.0))
    p.v(lambda e, z=z_ap: e.tensor_scalar_add(zc[:, :], z, negmean[:, 0:1]))
    p.s(lambda e, z=z_ap: e.activation(z, zc[:, :], AF.Square))
    p.v(lambda e, z=z_ap: e.reduce_sum(sm[:, 0:1], z, axis=AXX))
    p.v(lambda e: e.tensor_scalar(stdv[:, 0:1], sm[:, 0:1], 1.0 / 4096.0, 1e-5,
                                  op0=ALU.mult, op1=ALU.add))
    p.s(lambda e: e.activation(stdv[:, 0:1], stdv[:, 0:1], AF.Sqrt))
    p.v(lambda e: e.reciprocal(rstd[:, 0:1], stdv[:, 0:1]))
    p.v(lambda e, o=out_f32_tile: e.tensor_scalar_mul(o, zc[:, :], rstd[:, 0:1]))
    if out_bf_tile is not None:
        p.s(lambda e, o=out_bf_tile, i=out_f32_tile: e.copy(o, i))


def _build_l0_attn():
    nc = bass.Bass("TRN2", target_bir_lowering=False)
    xT = nc.dram_tensor("xT", [4096, 8192], F32, kind="ExternalInput").ap()
    wqT = nc.dram_tensor("wqT", [4096, 64], F32, kind="ExternalInput").ap()
    wkT = nc.dram_tensor("wkT", [4096, 64], F32, kind="ExternalInput").ap()
    wvT = nc.dram_tensor("wvT", [4096, 64], F32, kind="ExternalInput").ap()
    bq = nc.dram_tensor("bq", [64, 1], F32, kind="ExternalInput").ap()
    bk = nc.dram_tensor("bk", [64, 1], F32, kind="ExternalInput").ap()
    bv = nc.dram_tensor("bv", [64, 1], F32, kind="ExternalInput").ap()
    ident = nc.dram_tensor("ident", [128, 128], BF16, kind="ExternalInput").ap()
    oT = nc.dram_tensor("oT", [64, 8192], BF16, kind="ExternalOutput").ap()

    p = Prog()
    with contextlib.ExitStack() as st:
        en = st.enter_context
        wq_b = en(nc.sbuf_tensor("wq_b", [128, 32, 64], BF16))
        wk_b = en(nc.sbuf_tensor("wk_b", [128, 32, 64], BF16))
        wv_b = en(nc.sbuf_tensor("wv_b", [128, 32, 64], BF16))
        stg = en(nc.sbuf_tensor("stg", [128, 32, 64], F32))
        id_sb = en(nc.sbuf_tensor("id_sb", [128, 128], BF16))
        bq_s = en(nc.sbuf_tensor("bq_s", [64, 1], F32))
        bk_s = en(nc.sbuf_tensor("bk_s", [64, 1], F32))
        bv_s = en(nc.sbuf_tensor("bv_s", [64, 1], F32))
        xstg = en(nc.sbuf_tensor("xstg", [128, 512], F32))
        xbf = en(nc.sbuf_tensor("xbf", [128, 512], BF16))
        qT = en(nc.sbuf_tensor("qT", [64, 8192], BF16))
        kT = en(nc.sbuf_tensor("kT", [64, 8192], BF16))
        vT = en(nc.sbuf_tensor("vT", [64, 8192], BF16))
        vn = en(nc.sbuf_tensor("vn", [128, 64, 64], BF16))
        s_sb = en(nc.sbuf_tensor("s_sb", [128, 8192], F32))
        p_f = en(nc.sbuf_tensor("p_f", [128, 8192], F32))
        p_b = en(nc.sbuf_tensor("p_b", [128, 8192], BF16))
        pT = en(nc.sbuf_tensor("pT", [128, 128], BF16))
        oT_t = en(nc.sbuf_tensor("oT_t", [64, 128], BF16))
        mx = en(nc.sbuf_tensor("mx", [128, 1], F32))
        ngm = en(nc.sbuf_tensor("ngm", [128, 1], F32))
        rc = en(nc.sbuf_tensor("rc", [128, 1], F32))
        psq = en(nc.psum_tensor("psq", [64, 512], F32))
        psk = en(nc.psum_tensor("psk", [64, 512], F32))
        psv = en(nc.psum_tensor("psv", [64, 512], F32))
        ps_s = en(nc.psum_tensor("ps_s", [128, 512], F32))
        ps_t = en(nc.psum_tensor("ps_t", [128, 128], BF16))
        ps_o = en(nc.psum_tensor("ps_o", [64, 128], F32))

        p.dma(id_sb[:, :], ident)
        p.dma(bq_s[:, :], bq, allow_slow_non_contiguous=True)
        p.dma(bk_s[:, :], bk, allow_slow_non_contiguous=True)
        p.dma(bv_s[:, :], bv, allow_slow_non_contiguous=True)
        for (w_d, w_s) in ((wqT, wq_b), (wkT, wk_b), (wvT, wv_b)):
            src3 = w_d.rearrange("(c p) e -> p c e", p=128)
            _split_dma(p, stg, src3, 32, 4)
            p.v(lambda e, d=w_s: e.tensor_copy(d[:, :, :], stg[:, :, :]))
        xT3 = xT.rearrange("(c p) n -> p c n", p=128)
        for nt in range(16):
            n0 = nt * 512
            for dc in range(32):
                p.dma(xstg[:, :], xT3[:, dc, n0:n0 + 512])
                p.v(lambda e: e.tensor_copy(xbf[:, :], xstg[:, :]))
                st_ = (dc == 0)
                sp_ = (dc == 31)
                p.t(lambda e, dc=dc, st_=st_, sp_=sp_: e.matmul(
                    psq[:, :], wq_b[:, dc, :], xbf[:, :], start=st_, stop=sp_))
                p.t(lambda e, dc=dc, st_=st_, sp_=sp_: e.matmul(
                    psk[:, :], wk_b[:, dc, :], xbf[:, :], start=st_, stop=sp_))
                p.t(lambda e, dc=dc, st_=st_, sp_=sp_: e.matmul(
                    psv[:, :], wv_b[:, dc, :], xbf[:, :], start=st_, stop=sp_))
            p.v(lambda e, n0=n0: e.tensor_scalar_add(qT[:, n0:n0 + 512], psq[:, :], bq_s[:, 0:1]))
            p.v(lambda e, n0=n0: e.tensor_scalar_add(kT[:, n0:n0 + 512], psk[:, :], bk_s[:, 0:1]))
            p.v(lambda e, n0=n0: e.tensor_scalar_add(vT[:, n0:n0 + 512], psv[:, :], bv_s[:, 0:1]))
        for t in range(64):
            p.t(lambda e, t=t: e.transpose(ps_t[:, 0:64], vT[:, t * 128:(t + 1) * 128], id_sb[0:64, 0:64]))
            p.s(lambda e, t=t: e.copy(vn[:, t, :], ps_t[:, 0:64]))
        for qt in range(64):
            q0 = qt * 128
            for kt in range(16):
                k0 = kt * 512
                p.t(lambda e, q0=q0, k0=k0: e.matmul(
                    ps_s[:, :], qT[:, q0:q0 + 128], kT[:, k0:k0 + 512], start=True, stop=True))
                p.s(lambda e, k0=k0: e.copy(s_sb[:, k0:k0 + 512], ps_s[:, :]))
            p.v(lambda e: e.reduce_max(mx[:, 0:1], s_sb[:, :], axis=AXX))
            p.v(lambda e: e.tensor_scalar_mul(ngm[:, 0:1], mx[:, 0:1], -1.0))
            p.s(lambda e: e.activation(p_f[:, :], s_sb[:, :], AF.Exp, bias=ngm[:, 0:1]))
            p.v(lambda e: e.reduce_sum(mx[:, 0:1], p_f[:, :], axis=AXX))
            p.v(lambda e: e.reciprocal(rc[:, 0:1], mx[:, 0:1]))
            p.v(lambda e: e.tensor_scalar(p_b[:, :], p_f[:, :], rc[:, 0:1], INV_SQRT_512,
                                          op0=ALU.mult, op1=ALU.mult))
            for kt in range(64):
                p.t(lambda e, kt=kt: e.transpose(ps_t[:, :], p_b[:, kt * 128:(kt + 1) * 128], id_sb[:, :]))
                p.s(lambda e: e.copy(pT[:, :], ps_t[:, :]))
                p.t(lambda e, kt=kt: e.matmul(ps_o[:, :], vn[:, kt, :], pT[:, :],
                                              start=(kt == 0), stop=(kt == 63)))
            p.s(lambda e: e.copy(oT_t[:, :], ps_o[:, :]))
            p.dma(oT[:, q0:q0 + 128], oT_t[:, :])
        p.finish(nc)
    return nc


def _build_l0_proj():
    nc = bass.Bass("TRN2", target_bir_lowering=False)
    oTa = nc.dram_tensor("oTa", [512, 1024], BF16, kind="ExternalInput").ap()
    woT = nc.dram_tensor("woT", [512, 4096], F32, kind="ExternalInput").ap()
    wob = nc.dram_tensor("wob", [1, 4096], F32, kind="ExternalInput").ap()
    xr = nc.dram_tensor("xr", [1024, 4096], F32, kind="ExternalInput").ap()
    ones = nc.dram_tensor("ones", [1, 128], BF16, kind="ExternalInput").ap()
    y1 = nc.dram_tensor("y1", [1024, 4096], F32, kind="ExternalOutput").ap()
    y1b = nc.dram_tensor("y1b", [1024, 4096], BF16, kind="ExternalOutput").ap()

    p = Prog()
    with contextlib.ExitStack() as st:
        en = st.enter_context
        oT_sb = en(nc.sbuf_tensor("oT_sb", [128, 4, 1024], BF16))
        wo_f = en(nc.sbuf_tensor("wo_f", [128, 4, 4096], F32))
        wo_b = en(nc.sbuf_tensor("wo_b", [128, 4, 4096], BF16))
        wob_f = en(nc.sbuf_tensor("wob_f", [1, 4096], F32))
        wob_b = en(nc.sbuf_tensor("wob_b", [1, 4096], BF16))
        on_sb = en(nc.sbuf_tensor("on_sb", [1, 128], BF16))
        xr_t = en(nc.sbuf_tensor("xr_t", [128, 4096], F32))
        z = en(nc.sbuf_tensor("z", [128, 4096], F32))
        zc = en(nc.sbuf_tensor("zc", [128, 4096], F32))
        yf = en(nc.sbuf_tensor("yf", [128, 4096], F32))
        yb = en(nc.sbuf_tensor("yb", [128, 4096], BF16))
        sm = en(nc.sbuf_tensor("sm", [128, 1], F32))
        ngm = en(nc.sbuf_tensor("ngm", [128, 1], F32))
        stdv = en(nc.sbuf_tensor("stdv", [128, 1], F32))
        rstd = en(nc.sbuf_tensor("rstd", [128, 1], F32))
        ps = en(nc.psum_tensor("ps", [128, 512], F32))

        p.dma(on_sb[:, :], ones)
        p.dma(wob_f[:, :], wob)
        p.v(lambda e: e.tensor_copy(wob_b[:, :], wob_f[:, :]))
        oTa3 = oTa.rearrange("(c p) m -> p c m", p=128)
        p.dma(oT_sb[:, :, :], oTa3)
        woT3 = woT.rearrange("(c p) n -> p c n", p=128)
        p.dma(wo_f[:, :, :], woT3)
        p.v(lambda e: e.tensor_copy(wo_b[:, :, :], wo_f[:, :, :]))
        for mt in range(8):
            m0 = mt * 128
            p.dma(xr_t[:, :], xr[m0:m0 + 128, :])
            for ntl in range(8):
                n0 = ntl * 512
                for dc in range(4):
                    p.t(lambda e, dc=dc, m0=m0, n0=n0: e.matmul(
                        ps[:, :], oT_sb[:, dc, m0:m0 + 128], wo_b[:, dc, n0:n0 + 512],
                        start=(dc == 0), stop=False))
                p.t(lambda e, n0=n0: e.matmul(ps[:, :], on_sb[:, :], wob_b[:, n0:n0 + 512],
                                              start=False, stop=True))
                p.v(lambda e, n0=n0: e.tensor_add(z[:, n0:n0 + 512], ps[:, :], xr_t[:, n0:n0 + 512]))
            _layernorm_tail(p, z[:, :], zc, (sm, ngm, stdv, rstd), yf[:, :], yb[:, :])
            p.dma(y1[m0:m0 + 128, :], yf[:, :])
            p.dma(y1b[m0:m0 + 128, :], yb[:, :])
        p.finish(nc)
    return nc


def _dequant_group(p, qw_sb, unp_i, unp_f, qz_sb, z_i, z_f, s_f, ones_sb,
                   ps_z, ps_sc, W_dst, wtile, subw, nsub):
    unp3 = unp_i.rearrange("p (w i) -> p w i", i=8)
    z3 = z_i.rearrange("p (w i) -> p w i", i=8)
    for i in range(8):
        sh = 4 * ARO[i]
        p.v(lambda e, i=i, sh=sh: e.tensor_scalar(
            unp3[:, :, i], qw_sb[:, :], sh, 15, op0=ALU.logical_shift_right, op1=ALU.bitwise_and))
    p.v(lambda e: e.tensor_copy(unp_f[:, :], unp_i[:, :]))
    for i in range(8):
        sh = 4 * ARO[i]
        p.v(lambda e, i=i, sh=sh: e.tensor_scalar(
            z3[:, :, i], qz_sb[:, :], sh, 15, op0=ALU.logical_shift_right, op1=ALU.bitwise_and))
    p.v(lambda e: e.tensor_copy(z_f[:, :], z_i[:, :]))
    for sub in range(nsub):
        c0 = sub * subw
        p.t(lambda e, c0=c0: e.matmul(ps_z[:, :], ones_sb[:, :], z_f[:, c0:c0 + subw],
                                      start=True, stop=True))
        p.t(lambda e, c0=c0: e.matmul(ps_sc[:, :], ones_sb[:, :], s_f[:, c0:c0 + subw],
                                      start=True, stop=True))
        p.v(lambda e, c0=c0: e.tensor_sub(unp_f[:, c0:c0 + subw], unp_f[:, c0:c0 + subw], ps_z[:, :]))
        p.v(lambda e, c0=c0: e.tensor_tensor(W_dst[:, c0:c0 + subw], unp_f[:, c0:c0 + subw],
                                             ps_sc[:, :], op=ALU.mult))


def _build_ffn_gu():
    nc = bass.Bass("TRN2", target_bir_lowering=False)
    y1T = nc.dram_tensor("y1T", [4096, 8192], BF16, kind="ExternalInput").ap()
    gqw = nc.dram_tensor("gqw", [4096, 172], I32, kind="ExternalInput").ap()
    gsc = nc.dram_tensor("gsc", [32, 1376], F32, kind="ExternalInput").ap()
    gqz = nc.dram_tensor("gqz", [32, 172], I32, kind="ExternalInput").ap()
    uqw = nc.dram_tensor("uqw", [4096, 172], I32, kind="ExternalInput").ap()
    usc = nc.dram_tensor("usc", [32, 1376], F32, kind="ExternalInput").ap()
    uqz = nc.dram_tensor("uqz", [32, 172], I32, kind="ExternalInput").ap()
    onesf = nc.dram_tensor("onesf", [1, 128], F32, kind="ExternalInput").ap()
    h = nc.dram_tensor("h", [8192, 1376], BF16, kind="ExternalOutput").ap()

    p = Prog()
    with contextlib.ExitStack() as st:
        en = st.enter_context
        Wg = en(nc.sbuf_tensor("Wg", [128, 32, 688], BF16))
        Wu = en(nc.sbuf_tensor("Wu", [128, 32, 688], BF16))
        xt = en(nc.sbuf_tensor("xt", [128, 32, 128], BF16))
        qw_sb = en(nc.sbuf_tensor("qw_sb", [128, 86], I32))
        unp_i = en(nc.sbuf_tensor("unp_i", [128, 688], I32))
        unp_f = en(nc.sbuf_tensor("unp_f", [128, 688], F32))
        qz_sb = en(nc.sbuf_tensor("qz_sb", [1, 86], I32))
        z_i = en(nc.sbuf_tensor("z_i", [1, 688], I32))
        z_f = en(nc.sbuf_tensor("z_f", [1, 688], F32))
        s_f = en(nc.sbuf_tensor("s_f", [1, 688], F32))
        on_sb = en(nc.sbuf_tensor("on_sb", [1, 128], F32))
        sig = en(nc.sbuf_tensor("sig", [128, 344], F32))
        hf = en(nc.sbuf_tensor("hf", [128, 344], F32))
        hb = en(nc.sbuf_tensor("hb", [128, 344], BF16))
        ps_z = en(nc.psum_tensor("ps_z", [128, 344], F32))
        ps_sc = en(nc.psum_tensor("ps_sc", [128, 344], F32))
        ps_g = en(nc.psum_tensor("ps_g", [128, 344], F32))
        ps_u = en(nc.psum_tensor("ps_u", [128, 344], F32))

        p.dma(on_sb[:, :], onesf)
        y1T3 = y1T.rearrange("(c p) n -> p c n", p=128)
        for h2 in range(2):
            w0 = 86 * h2
            c0h = 688 * h2
            for (qwd, scd, qzd, Wdst) in ((gqw, gsc, gqz, Wg), (uqw, usc, uqz, Wu)):
                for g in range(32):
                    p.dma(qw_sb[:, :], qwd[g * 128:(g + 1) * 128, w0:w0 + 86])
                    p.dma(qz_sb[:, :], qzd[g:g + 1, w0:w0 + 86])
                    p.dma(s_f[:, :], scd[g:g + 1, c0h:c0h + 688])
                    Wrow = Wdst[:, g, :]
                    _dequant_group(p, qw_sb, unp_i, unp_f, qz_sb, z_i, z_f, s_f,
                                   on_sb, ps_z, ps_sc, Wrow, None, 344, 2)
            for mt in range(64):
                m0 = mt * 128
                _split_dma(p, xt, y1T3[:, :, m0:m0 + 128], 32, 4)
                for sub in range(2):
                    c0 = sub * 344
                    for dc in range(32):
                        p.t(lambda e, dc=dc, c0=c0: e.matmul(
                            ps_g[:, :], xt[:, dc, :], Wg[:, dc, c0:c0 + 344],
                            start=(dc == 0), stop=(dc == 31)))
                    for dc in range(32):
                        p.t(lambda e, dc=dc, c0=c0: e.matmul(
                            ps_u[:, :], xt[:, dc, :], Wu[:, dc, c0:c0 + 344],
                            start=(dc == 0), stop=(dc == 31)))
                    p.s(lambda e: e.activation(sig[:, :], ps_g[:, :], AF.Sigmoid))
                    p.v(lambda e: e.tensor_mul(hf[:, :], sig[:, :], ps_g[:, :]))
                    p.v(lambda e: e.tensor_mul(hb[:, :], hf[:, :], ps_u[:, :]))
                    p.dma(h[m0:m0 + 128, c0h + c0:c0h + c0 + 344], hb[:, :])
        p.finish(nc)
    return nc


def _build_ffn_down():
    nc = bass.Bass("TRN2", target_bir_lowering=False)
    hT = nc.dram_tensor("hT", [11008, 8192], BF16, kind="ExternalInput").ap()
    dqw = nc.dram_tensor("dqw", [11008, 64], I32, kind="ExternalInput").ap()
    dsc = nc.dram_tensor("dsc", [86, 512], F32, kind="ExternalInput").ap()
    dqz = nc.dram_tensor("dqz", [86, 64], I32, kind="ExternalInput").ap()
    onesf = nc.dram_tensor("onesf", [1, 128], F32, kind="ExternalInput").ap()
    fo = nc.dram_tensor("fo", [8192, 512], F32, kind="ExternalOutput").ap()

    p = Prog()
    with contextlib.ExitStack() as st:
        en = st.enter_context
        Wd = en(nc.sbuf_tensor("Wd", [128, 86, 512], BF16))
        ht = en(nc.sbuf_tensor("ht", [128, 86, 128], BF16))
        qw_sb = en(nc.sbuf_tensor("qw_sb", [128, 64], I32))
        unp_i = en(nc.sbuf_tensor("unp_i", [128, 512], I32))
        unp_f = en(nc.sbuf_tensor("unp_f", [128, 512], F32))
        qz_sb = en(nc.sbuf_tensor("qz_sb", [1, 64], I32))
        z_i = en(nc.sbuf_tensor("z_i", [1, 512], I32))
        z_f = en(nc.sbuf_tensor("z_f", [1, 512], F32))
        s_f = en(nc.sbuf_tensor("s_f", [1, 512], F32))
        on_sb = en(nc.sbuf_tensor("on_sb", [1, 128], F32))
        fo_t = en(nc.sbuf_tensor("fo_t", [128, 512], F32))
        ps_z = en(nc.psum_tensor("ps_z", [128, 512], F32))
        ps_sc = en(nc.psum_tensor("ps_sc", [128, 512], F32))
        ps = en(nc.psum_tensor("ps", [128, 512], F32))

        p.dma(on_sb[:, :], onesf)
        for g in range(86):
            p.dma(qw_sb[:, :], dqw[g * 128:(g + 1) * 128, :])
            p.dma(qz_sb[:, :], dqz[g:g + 1, :])
            p.dma(s_f[:, :], dsc[g:g + 1, :])
            _dequant_group(p, qw_sb, unp_i, unp_f, qz_sb, z_i, z_f, s_f,
                           on_sb, ps_z, ps_sc, Wd[:, g, :], None, 512, 1)
        hT3 = hT.rearrange("(c p) n -> p c n", p=128)
        for mt in range(64):
            m0 = mt * 128
            _split_dma(p, ht, hT3[:, :, m0:m0 + 128], 86, 9)
            for dc in range(86):
                p.t(lambda e, dc=dc: e.matmul(ps[:, :], ht[:, dc, :], Wd[:, dc, :],
                                              start=(dc == 0), stop=(dc == 85)))
            p.s(lambda e: e.copy(fo_t[:, :], ps[:, :]))
            p.dma(fo[m0:m0 + 128, :], fo_t[:, :])
        p.finish(nc)
    return nc


def _build_res_ln():
    nc = bass.Bass("TRN2", target_bir_lowering=False)
    a = nc.dram_tensor("a", [1024, 4096], F32, kind="ExternalInput").ap()
    b = nc.dram_tensor("b", [1024, 4096], F32, kind="ExternalInput").ap()
    y = nc.dram_tensor("y", [1024, 4096], F32, kind="ExternalOutput").ap()
    yb = nc.dram_tensor("yb", [1024, 4096], BF16, kind="ExternalOutput").ap()

    p = Prog()
    with contextlib.ExitStack() as st:
        en = st.enter_context
        at = en(nc.sbuf_tensor("at", [128, 4096], F32))
        bt = en(nc.sbuf_tensor("bt", [128, 4096], F32))
        z = en(nc.sbuf_tensor("z", [128, 4096], F32))
        zc = en(nc.sbuf_tensor("zc", [128, 4096], F32))
        yf = en(nc.sbuf_tensor("yf", [128, 4096], F32))
        ybt = en(nc.sbuf_tensor("ybt", [128, 4096], BF16))
        sm = en(nc.sbuf_tensor("sm", [128, 1], F32))
        ngm = en(nc.sbuf_tensor("ngm", [128, 1], F32))
        stdv = en(nc.sbuf_tensor("stdv", [128, 1], F32))
        rstd = en(nc.sbuf_tensor("rstd", [128, 1], F32))
        for mt in range(8):
            m0 = mt * 128
            p.dma(at[:, :], a[m0:m0 + 128, :])
            p.dma(bt[:, :], b[m0:m0 + 128, :])
            p.v(lambda e: e.tensor_add(z[:, :], at[:, :], bt[:, :]))
            _layernorm_tail(p, z[:, :], zc, (sm, ngm, stdv, rstd), yf[:, :], ybt[:, :])
            p.dma(y[m0:m0 + 128, :], yf[:, :])
            p.dma(yb[m0:m0 + 128, :], ybt[:, :])
        p.finish(nc)
    return nc


def _build_l1_attn():
    nc = bass.Bass("TRN2", target_bir_lowering=False)
    y2T = nc.dram_tensor("y2T", [4096, 8192], BF16, kind="ExternalInput").ap()
    wqTc = nc.dram_tensor("wqTc", [4096, 512], F32, kind="ExternalInput").ap()
    wkTc = nc.dram_tensor("wkTc", [4096, 128], F32, kind="ExternalInput").ap()
    wvTc = nc.dram_tensor("wvTc", [4096, 128], F32, kind="ExternalInput").ap()
    ident = nc.dram_tensor("ident", [128, 128], BF16, kind="ExternalInput").ap()
    tri = nc.dram_tensor("tri", [128, 128], F32, kind="ExternalInput").ap()
    upp = nc.dram_tensor("upp", [128, 128], F32, kind="ExternalInput").ap()
    oTc = nc.dram_tensor("oTc", [512, 8192], BF16, kind="ExternalOutput").ap()

    p = Prog()
    with contextlib.ExitStack() as st:
        en = st.enter_context
        wq_b = en(nc.sbuf_tensor("wq_b", [128, 32, 512], BF16))
        wk_b = en(nc.sbuf_tensor("wk_b", [128, 32, 128], BF16))
        wv_b = en(nc.sbuf_tensor("wv_b", [128, 32, 128], BF16))
        stg = en(nc.sbuf_tensor("stg", [128, 512], F32))
        ystg = en(nc.sbuf_tensor("ystg", [128, 512], BF16))
        id_sb = en(nc.sbuf_tensor("id_sb", [128, 128], BF16))
        tri_s = en(nc.sbuf_tensor("tri_s", [128, 128], F32))
        upp_s = en(nc.sbuf_tensor("upp_s", [128, 128], F32))
        qT = en(nc.sbuf_tensor("qT", [128, 4, 8192], BF16))
        kT = en(nc.sbuf_tensor("kT", [128, 8192], BF16))
        vT = en(nc.sbuf_tensor("vT", [128, 8192], BF16))
        vn = en(nc.sbuf_tensor("vn", [128, 64, 128], BF16))
        s_sb = en(nc.sbuf_tensor("s_sb", [128, 640], F32))
        p_f = en(nc.sbuf_tensor("p_f", [128, 640], F32))
        p_b = en(nc.sbuf_tensor("p_b", [128, 640], BF16))
        pT = en(nc.sbuf_tensor("pT", [128, 128], BF16))
        oT_t = en(nc.sbuf_tensor("oT_t", [128, 128], BF16))
        mx = en(nc.sbuf_tensor("mx", [128, 1], F32))
        ngm = en(nc.sbuf_tensor("ngm", [128, 1], F32))
        rc = en(nc.sbuf_tensor("rc", [128, 1], F32))
        psq = [en(nc.psum_tensor("psq%d" % i, [128, 512], F32)) for i in range(4)]
        psk = en(nc.psum_tensor("psk", [128, 512], F32))
        psv = en(nc.psum_tensor("psv", [128, 512], F32))
        ps_t = en(nc.psum_tensor("ps_t", [128, 128], BF16))

        ps_s = psq[0]
        ps_o = psq[1]

        p.dma(id_sb[:, :], ident)
        p.dma(tri_s[:, :], tri)
        p.dma(upp_s[:, :], upp)
        for dc in range(32):
            p.dma(stg[:, :], wqTc[dc * 128:(dc + 1) * 128, :])
            p.v(lambda e, dc=dc: e.tensor_copy(wq_b[:, dc, :], stg[:, :]))
            p.dma(stg[:, 0:128], wkTc[dc * 128:(dc + 1) * 128, :])
            p.v(lambda e, dc=dc: e.tensor_copy(wk_b[:, dc, :], stg[:, 0:128]))
            p.dma(stg[:, 0:128], wvTc[dc * 128:(dc + 1) * 128, :])
            p.v(lambda e, dc=dc: e.tensor_copy(wv_b[:, dc, :], stg[:, 0:128]))
        y2T3 = y2T.rearrange("(c p) n -> p c n", p=128)
        for nt in range(16):
            n0 = nt * 512
            for dc in range(32):
                p.dma(ystg[:, :], y2T3[:, dc, n0:n0 + 512])
                st_ = (dc == 0)
                sp_ = (dc == 31)
                for hh in range(4):
                    p.t(lambda e, dc=dc, hh=hh, st_=st_, sp_=sp_: e.matmul(
                        psq[hh][:, :], wq_b[:, dc, hh * 128:(hh + 1) * 128], ystg[:, :],
                        start=st_, stop=sp_))
                p.t(lambda e, dc=dc, st_=st_, sp_=sp_: e.matmul(
                    psk[:, :], wk_b[:, dc, :], ystg[:, :], start=st_, stop=sp_))
                p.t(lambda e, dc=dc, st_=st_, sp_=sp_: e.matmul(
                    psv[:, :], wv_b[:, dc, :], ystg[:, :], start=st_, stop=sp_))
            for hh in range(4):
                p.s(lambda e, hh=hh, n0=n0: e.copy(qT[:, hh, n0:n0 + 512], psq[hh][:, :]))
            p.s(lambda e, n0=n0: e.copy(kT[:, n0:n0 + 512], psk[:, :]))
            p.s(lambda e, n0=n0: e.copy(vT[:, n0:n0 + 512], psv[:, :]))
        for t in range(64):
            p.t(lambda e, t=t: e.transpose(ps_t[:, :], vT[:, t * 128:(t + 1) * 128], id_sb[:, :]))
            p.s(lambda e, t=t: e.copy(vn[:, t, :], ps_t[:, :]))
        for qb in range(64):
            q0 = qb * 128
            t0 = max(0, qb - 4)
            ntk = qb - t0 + 1
            w = ntk * 128
            for hh in range(4):
                for j in range(ntk):
                    kt = t0 + j
                    p.t(lambda e, hh=hh, q0=q0, kt=kt: e.matmul(
                        ps_s[:, 0:128], qT[:, hh, q0:q0 + 128], kT[:, kt * 128:(kt + 1) * 128],
                        start=True, stop=True, skip_group_check=True))
                    if kt == qb:
                        p.v(lambda e, j=j: e.scalar_tensor_tensor(
                            s_sb[:, j * 128:(j + 1) * 128], ps_s[:, 0:128], INV_SQRT_128,
                            tri_s[:, :], op0=ALU.mult, op1=ALU.add))
                    elif kt == qb - 4:
                        p.v(lambda e, j=j: e.scalar_tensor_tensor(
                            s_sb[:, j * 128:(j + 1) * 128], ps_s[:, 0:128], INV_SQRT_128,
                            upp_s[:, :], op0=ALU.mult, op1=ALU.add))
                    else:
                        p.v(lambda e, j=j: e.tensor_scalar_mul(
                            s_sb[:, j * 128:(j + 1) * 128], ps_s[:, 0:128], INV_SQRT_128))
                p.v(lambda e, w=w: e.reduce_max(mx[:, 0:1], s_sb[:, 0:w], axis=AXX))
                p.v(lambda e: e.tensor_scalar_mul(ngm[:, 0:1], mx[:, 0:1], -1.0))
                p.s(lambda e, w=w: e.activation(p_f[:, 0:w], s_sb[:, 0:w], AF.Exp, bias=ngm[:, 0:1]))
                p.v(lambda e, w=w: e.reduce_sum(mx[:, 0:1], p_f[:, 0:w], axis=AXX))
                p.v(lambda e: e.reciprocal(rc[:, 0:1], mx[:, 0:1]))
                p.v(lambda e, w=w: e.tensor_scalar_mul(p_b[:, 0:w], p_f[:, 0:w], rc[:, 0:1]))
                for j in range(ntk):
                    kt = t0 + j
                    p.t(lambda e, j=j: e.transpose(ps_t[:, :], p_b[:, j * 128:(j + 1) * 128], id_sb[:, :]))
                    p.s(lambda e: e.copy(pT[:, :], ps_t[:, :]))
                    p.t(lambda e, kt=kt, j=j, ntk=ntk: e.matmul(
                        ps_o[:, 0:128], vn[:, kt, :], pT[:, :],
                        start=(j == 0), stop=(j == ntk - 1), skip_group_check=True))
                p.s(lambda e: e.copy(oT_t[:, :], ps_o[:, 0:128]))
                p.dma(oTc[hh * 128:(hh + 1) * 128, q0:q0 + 128], oT_t[:, :])
        p.finish(nc)
    return nc


def _build_proj_res_ln(rhs_name, rhs_is_sign):
    nc = bass.Bass("TRN2", target_bir_lowering=False)
    lT = nc.dram_tensor("lT", [4096, 1024], BF16, kind="ExternalInput").ap()
    wf = nc.dram_tensor("wf", [4096, 4096], F32, kind="ExternalInput").ap()
    xres = nc.dram_tensor("xres", [1024, 4096], F32, kind="ExternalInput").ap()
    if rhs_is_sign:
        scl = nc.dram_tensor("scl", [1, 1], F32, kind="ExternalInput").ap()
        onesf = nc.dram_tensor("onesf", [1, 128], F32, kind="ExternalInput").ap()
    y = nc.dram_tensor("y", [1024, 4096], F32, kind="ExternalOutput").ap()
    yb = nc.dram_tensor("yb", [1024, 4096], BF16, kind="ExternalOutput").ap()

    p = Prog()
    with contextlib.ExitStack() as st:
        en = st.enter_context
        w_b = en(nc.sbuf_tensor("w_b", [128, 32, 512], BF16))
        stg = en(nc.sbuf_tensor("stg", [128, 512], F32))
        l_mt = en(nc.sbuf_tensor("l_mt", [128, 32, 128], BF16))
        z = en(nc.sbuf_tensor("z", [128, 4, 4096], F32))
        xr_t = en(nc.sbuf_tensor("xr_t", [128, 4096], F32))
        zc = en(nc.sbuf_tensor("zc", [128, 4096], F32))
        yf = en(nc.sbuf_tensor("yf", [128, 4096], F32))
        ybt = en(nc.sbuf_tensor("ybt", [128, 4096], BF16))
        sm = en(nc.sbuf_tensor("sm", [128, 1], F32))
        ngm = en(nc.sbuf_tensor("ngm", [128, 1], F32))
        stdv = en(nc.sbuf_tensor("stdv", [128, 1], F32))
        rstd = en(nc.sbuf_tensor("rstd", [128, 1], F32))
        ps = en(nc.psum_tensor("ps", [128, 512], F32))
        if rhs_is_sign:
            scl_s = en(nc.sbuf_tensor("scl_s", [1, 1], F32))
            sp1 = en(nc.sbuf_tensor("sp1", [1, 1], F32))
            sp2 = en(nc.sbuf_tensor("sp2", [1, 1], F32))
            spb = en(nc.sbuf_tensor("spb", [128, 1], F32))
            on_sb = en(nc.sbuf_tensor("on_sb", [1, 128], F32))
            ps_sp = en(nc.psum_tensor("ps_sp", [128, 1], F32))
            p.dma(on_sb[:, :], onesf)
            p.dma(scl_s[:, :], scl, allow_slow_non_contiguous=True)
            p.s(lambda e: e.activation(sp1[:, :], scl_s[:, :], AF.Abs))
            p.v(lambda e: e.tensor_scalar_mul(sp1[:, :], sp1[:, :], -1.0))
            p.s(lambda e: e.activation(sp1[:, :], sp1[:, :], AF.Exp))
            p.v(lambda e: e.tensor_scalar_add(sp1[:, :], sp1[:, :], 1.0))
            p.s(lambda e: e.activation(sp1[:, :], sp1[:, :], AF.Ln))
            p.v(lambda e: e.tensor_scalar_max(sp2[:, :], scl_s[:, :], 0.0))
            p.v(lambda e: e.tensor_add(sp1[:, :], sp1[:, :], sp2[:, :]))
            p.t(lambda e: e.matmul(ps_sp[:, :], on_sb[:, :], sp1[:, :], start=True, stop=True))
            p.v(lambda e: e.tensor_copy(spb[:, :], ps_sp[:, :]))
        lT3 = lT.rearrange("(c p) n -> p c n", p=128)
        for half in range(2):
            for ntl in range(8):
                n0 = ntl * 512
                for dc in range(32):
                    p.dma(stg[:, :], wf[dc * 128:(dc + 1) * 128, n0:n0 + 512])
                    p.v(lambda e, dc=dc: e.tensor_copy(w_b[:, dc, :], stg[:, :]))
                for mt4 in range(4):
                    mtg = half * 4 + mt4
                    m0 = mtg * 128
                    _split_dma(p, l_mt, lT3[:, :, m0:m0 + 128], 32, 4)
                    for dc in range(32):
                        p.t(lambda e, dc=dc: e.matmul(ps[:, :], l_mt[:, dc, :], w_b[:, dc, :],
                                                      start=(dc == 0), stop=(dc == 31)))
                    if rhs_is_sign:
                        p.v(lambda e, mt4=mt4, n0=n0: e.tensor_scalar_mul(
                            z[:, mt4, n0:n0 + 512], ps[:, :], spb[:, 0:1]))
                    else:
                        p.v(lambda e, mt4=mt4, n0=n0: e.tensor_copy(z[:, mt4, n0:n0 + 512], ps[:, :]))
            for mt4 in range(4):
                mtg = half * 4 + mt4
                m0 = mtg * 128
                p.dma(xr_t[:, :], xres[m0:m0 + 128, :])
                p.v(lambda e, mt4=mt4: e.tensor_add(z[:, mt4, :], z[:, mt4, :], xr_t[:, :]))
                _layernorm_tail(p, z[:, mt4, :], zc, (sm, ngm, stdv, rstd), yf[:, :], ybt[:, :])
                p.dma(y[m0:m0 + 128, :], yf[:, :])
                p.dma(yb[m0:m0 + 128, :], ybt[:, :])
        p.finish(nc)
    return nc


def _run(nc, in_maps):
    res = run_bass_kernel_spmd(nc, in_maps, core_ids=list(range(8)))
    return res.results


def kernel(**inputs):
    f32 = np.float32
    x = np.ascontiguousarray(inputs["L0_attn_x"][0])
    xT = np.ascontiguousarray(x.T)

    ident = np.eye(128, dtype=BFD)
    ones_b = np.ones((1, 128), dtype=BFD)
    ones_f = np.ones((1, 128), dtype=f32)
    ii = np.arange(128)[:, None]
    jj = np.arange(128)[None, :]
    tri = np.where(jj <= ii, 0.0, NEG).astype(f32)
    upp = np.where(jj > ii, 0.0, NEG).astype(f32)

    wqk = inputs["L0_attn_Wqk_w"].reshape(8, 64, 2, 4096)
    bqk = inputs["L0_attn_Wqk_b"].reshape(8, 64, 2)
    wv = inputs["L0_attn_Wv_w"].reshape(8, 64, 4096)
    bv = inputs["L0_attn_Wv_b"].reshape(8, 64)
    nc1 = _build_l0_attn()
    maps = []
    for h in range(8):
        maps.append({
            "xT": xT,
            "wqT": np.ascontiguousarray(wqk[h, :, 0, :].T),
            "wkT": np.ascontiguousarray(wqk[h, :, 1, :].T),
            "wvT": np.ascontiguousarray(wv[h].T),
            "bq": np.ascontiguousarray(bqk[h, :, 0].reshape(64, 1)),
            "bk": np.ascontiguousarray(bqk[h, :, 1].reshape(64, 1)),
            "bv": np.ascontiguousarray(bv[h].reshape(64, 1)),
            "ident": ident,
        })
    r1 = _run(nc1, maps)
    oTa = np.concatenate([r1[h]["oT"] for h in range(8)], axis=0)

    woT = np.ascontiguousarray(inputs["L0_attn_Wo_w"].T)
    wob = np.ascontiguousarray(inputs["L0_attn_Wo_b"].reshape(1, 4096))
    nc2 = _build_l0_proj()
    maps = [{"oTa": np.ascontiguousarray(oTa[:, c * 1024:(c + 1) * 1024]),
             "woT": woT, "wob": wob, "ones": ones_b,
             "xr": np.ascontiguousarray(x[c * 1024:(c + 1) * 1024])} for c in range(8)]
    r2 = _run(nc2, maps)
    y1 = np.concatenate([r2[c]["y1"] for c in range(8)], axis=0)
    y1b = np.concatenate([r2[c]["y1b"] for c in range(8)], axis=0)
    y1T = np.ascontiguousarray(y1b.T)

    nc3 = _build_ffn_gu()
    maps = []
    for c in range(8):
        maps.append({
            "y1T": y1T,
            "gqw": np.ascontiguousarray(inputs["L0_ffn_gate_qweight"][:, c * 172:(c + 1) * 172]),
            "gsc": np.ascontiguousarray(inputs["L0_ffn_gate_scales"][:, c * 1376:(c + 1) * 1376]),
            "gqz": np.ascontiguousarray(inputs["L0_ffn_gate_qzeros"][:, c * 172:(c + 1) * 172]),
            "uqw": np.ascontiguousarray(inputs["L0_ffn_up_qweight"][:, c * 172:(c + 1) * 172]),
            "usc": np.ascontiguousarray(inputs["L0_ffn_up_scales"][:, c * 1376:(c + 1) * 1376]),
            "uqz": np.ascontiguousarray(inputs["L0_ffn_up_qzeros"][:, c * 172:(c + 1) * 172]),
            "onesf": ones_f,
        })
    r3 = _run(nc3, maps)
    hfull = np.concatenate([r3[c]["h"] for c in range(8)], axis=1)
    hT = np.ascontiguousarray(hfull.T)

    nc4 = _build_ffn_down()
    maps = []
    for c in range(8):
        maps.append({
            "hT": hT,
            "dqw": np.ascontiguousarray(inputs["L0_ffn_down_qweight"][:, c * 64:(c + 1) * 64]),
            "dsc": np.ascontiguousarray(inputs["L0_ffn_down_scales"][:, c * 512:(c + 1) * 512]),
            "dqz": np.ascontiguousarray(inputs["L0_ffn_down_qzeros"][:, c * 64:(c + 1) * 64]),
            "onesf": ones_f,
        })
    r4 = _run(nc4, maps)
    ffo = np.concatenate([r4[c]["fo"] for c in range(8)], axis=1)

    nc5 = _build_res_ln()
    maps = [{"a": np.ascontiguousarray(y1[c * 1024:(c + 1) * 1024]),
             "b": np.ascontiguousarray(ffo[c * 1024:(c + 1) * 1024])} for c in range(8)]
    r5 = _run(nc5, maps)
    y2 = np.concatenate([r5[c]["y"] for c in range(8)], axis=0)
    y2b = np.concatenate([r5[c]["yb"] for c in range(8)], axis=0)
    y2T = np.ascontiguousarray(y2b.T)

    wqT_full = np.ascontiguousarray(inputs["L1_attn_wq"].T)
    wkT_full = np.ascontiguousarray(inputs["L1_attn_wk"].T)
    wvT_full = np.ascontiguousarray(inputs["L1_attn_wv"].T)
    nc6 = _build_l1_attn()
    maps = []
    for c in range(8):
        maps.append({
            "y2T": y2T,
            "wqTc": np.ascontiguousarray(wqT_full[:, c * 512:(c + 1) * 512]),
            "wkTc": np.ascontiguousarray(wkT_full[:, c * 128:(c + 1) * 128]),
            "wvTc": np.ascontiguousarray(wvT_full[:, c * 128:(c + 1) * 128]),
            "ident": ident, "tri": tri, "upp": upp,
        })
    r6 = _run(nc6, maps)
    oT1 = np.concatenate([r6[c]["oTc"] for c in range(8)], axis=0)

    woT1 = np.ascontiguousarray(inputs["L1_attn_wo"].T)
    nc7 = _build_proj_res_ln("wo", False)
    maps = [{"lT": np.ascontiguousarray(oT1[:, c * 1024:(c + 1) * 1024]), "wf": woT1,
             "xres": np.ascontiguousarray(y2[c * 1024:(c + 1) * 1024])} for c in range(8)]
    r7 = _run(nc7, maps)
    y3 = np.concatenate([r7[c]["y"] for c in range(8)], axis=0)
    y3b = np.concatenate([r7[c]["yb"] for c in range(8)], axis=0)
    y3T = np.ascontiguousarray(y3b.T)

    sgn = np.ascontiguousarray(inputs["L1_ffn_sign"])
    scl = np.asarray(inputs["L1_ffn_scale"], dtype=f32).reshape(1, 1)
    nc8 = _build_proj_res_ln("sign", True)
    maps = [{"lT": np.ascontiguousarray(y3T[:, c * 1024:(c + 1) * 1024]),
             "wf": sgn, "scl": scl, "onesf": ones_f,
             "xres": np.ascontiguousarray(y3[c * 1024:(c + 1) * 1024])} for c in range(8)]
    r8 = _run(nc8, maps)
    out = np.concatenate([r8[c]["y"] for c in range(8)], axis=0)
    return out.reshape(1, 8192, 4096).astype(np.float32)


# version 02 (final)
# speedup vs baseline: 1.3023x; 1.3023x over previous
import contextlib
import numpy as np
import ml_dtypes
import concourse.bass as bass
import concourse.mybir as mybir
from concourse.bass_utils import run_bass_kernel_spmd

F32 = mybir.dt.float32
BF16 = mybir.dt.bfloat16
I32 = mybir.dt.int32
AF = mybir.ActivationFunctionType
ALU = mybir.AluOpType
AXX = mybir.AxisListType.X
BFD = ml_dtypes.bfloat16

NEG = -1.0e30
ARO = [0, 4, 1, 5, 2, 6, 3, 7]
INV_SQRT_512 = 1.0 / float(np.sqrt(512.0))
INV_SQRT_128 = 1.0 / float(np.sqrt(128.0))


class Prog:

    def __init__(self):
        self.ops = []

    def dma(self, out, in_, **kw):
        self.ops.append(
            ("sync", (lambda e, o=out, i=in_, k=dict(kw): e.dma_start(out=o, in_=i, **k)), 16)
        )

    def t(self, fn):
        self.ops.append(("tensor", fn, 1))

    def tmm(self, fn):
        self.ops.append(("tensor#nowait", fn, 1))

    def v(self, fn):
        self.ops.append(("vector", fn, 1))

    def s(self, fn):
        self.ops.append(("scalar", fn, 1))

    def finish(self, nc):
        cum = [0]
        for (_, _, inc) in self.ops:
            cum.append(cum[-1] + inc)
        total = cum[-1]
        with nc.semaphore("SER") as sem, nc.Block() as block:
            def mk(name):
                def body(eng):
                    for i, (en, fn, inc) in enumerate(self.ops):
                        base = en.split("#")[0]
                        if base != name:
                            continue
                        nowait = (en.endswith("#nowait") and i > 0
                                  and self.ops[i - 1][0].split("#")[0] == "tensor")
                        if not nowait:
                            eng.wait_ge(sem, cum[i])
                        fn(eng).then_inc(sem, inc)
                    eng.wait_ge(sem, total)
                return body
            block.sync(mk("sync"))
            block.tensor(mk("tensor"))
            block.vector(mk("vector"))
            block.scalar(mk("scalar"))


def _split_dma(p, dst3, src3, nchunks, split):
    step = (nchunks + split - 1) // split
    for c0 in range(0, nchunks, step):
        c1 = min(c0 + step, nchunks)
        p.dma(dst3[:, c0:c1, :], src3[:, c0:c1, :])


def _layernorm_tail(p, z_ap, zc, tmps, out_f32_tile, out_bf_tile=None):
    sm, negmean, stdv, rstd = tmps
    p.v(lambda e, z=z_ap: e.reduce_sum(sm[:, 0:1], z, axis=AXX))
    p.v(lambda e: e.tensor_scalar_mul(negmean[:, 0:1], sm[:, 0:1], -1.0 / 4096.0))
    p.v(lambda e, z=z_ap: e.tensor_scalar_add(zc[:, :], z, negmean[:, 0:1]))
    p.s(lambda e, z=z_ap: e.activation(z, zc[:, :], AF.Square))
    p.v(lambda e, z=z_ap: e.reduce_sum(sm[:, 0:1], z, axis=AXX))
    p.v(lambda e: e.tensor_scalar(stdv[:, 0:1], sm[:, 0:1], 1.0 / 4096.0, 1e-5,
                                  op0=ALU.mult, op1=ALU.add))
    p.s(lambda e: e.activation(stdv[:, 0:1], stdv[:, 0:1], AF.Sqrt))
    p.v(lambda e: e.reciprocal(rstd[:, 0:1], stdv[:, 0:1]))
    p.v(lambda e, o=out_f32_tile: e.tensor_scalar_mul(o, zc[:, :], rstd[:, 0:1]))
    if out_bf_tile is not None:
        p.s(lambda e, o=out_bf_tile, i=out_f32_tile: e.copy(o, i))


def _build_l0_attn():
    nc = bass.Bass("TRN2", target_bir_lowering=False)
    xT = nc.dram_tensor("xT", [4096, 8192], F32, kind="ExternalInput").ap()
    wqT = nc.dram_tensor("wqT", [4096, 64], F32, kind="ExternalInput").ap()
    wkT = nc.dram_tensor("wkT", [4096, 64], F32, kind="ExternalInput").ap()
    wvT = nc.dram_tensor("wvT", [4096, 64], F32, kind="ExternalInput").ap()
    bq = nc.dram_tensor("bq", [64, 1], F32, kind="ExternalInput").ap()
    bk = nc.dram_tensor("bk", [64, 1], F32, kind="ExternalInput").ap()
    bv = nc.dram_tensor("bv", [64, 1], F32, kind="ExternalInput").ap()
    ident = nc.dram_tensor("ident", [128, 128], BF16, kind="ExternalInput").ap()
    oT = nc.dram_tensor("oT", [64, 8192], BF16, kind="ExternalOutput").ap()

    p = Prog()
    with contextlib.ExitStack() as st:
        en = st.enter_context
        wq_b = en(nc.sbuf_tensor("wq_b", [128, 32, 64], BF16))
        wk_b = en(nc.sbuf_tensor("wk_b", [128, 32, 64], BF16))
        wv_b = en(nc.sbuf_tensor("wv_b", [128, 32, 64], BF16))
        stg = en(nc.sbuf_tensor("stg", [128, 32, 64], F32))
        id_sb = en(nc.sbuf_tensor("id_sb", [128, 128], BF16))
        bq_s = en(nc.sbuf_tensor("bq_s", [64, 1], F32))
        bk_s = en(nc.sbuf_tensor("bk_s", [64, 1], F32))
        bv_s = en(nc.sbuf_tensor("bv_s", [64, 1], F32))
        xstg = en(nc.sbuf_tensor("xstg", [128, 512], F32))
        xbf = en(nc.sbuf_tensor("xbf", [128, 512], BF16))
        qT = en(nc.sbuf_tensor("qT", [64, 8192], BF16))
        kT = en(nc.sbuf_tensor("kT", [64, 8192], BF16))
        vT = en(nc.sbuf_tensor("vT", [64, 8192], BF16))
        vn = en(nc.sbuf_tensor("vn", [128, 64, 64], BF16))
        s_sb = en(nc.sbuf_tensor("s_sb", [128, 8192], F32))
        p_f = en(nc.sbuf_tensor("p_f", [128, 8192], F32))
        p_b = en(nc.sbuf_tensor("p_b", [128, 8192], BF16))
        pT = en(nc.sbuf_tensor("pT", [128, 128], BF16))
        oT_t = en(nc.sbuf_tensor("oT_t", [64, 128], BF16))
        mx = en(nc.sbuf_tensor("mx", [128, 1], F32))
        ngm = en(nc.sbuf_tensor("ngm", [128, 1], F32))
        rc = en(nc.sbuf_tensor("rc", [128, 1], F32))
        psq = en(nc.psum_tensor("psq", [64, 512], F32))
        psk = en(nc.psum_tensor("psk", [64, 512], F32))
        psv = en(nc.psum_tensor("psv", [64, 512], F32))
        ps_s = en(nc.psum_tensor("ps_s", [128, 512], F32))
        ps_t = en(nc.psum_tensor("ps_t", [128, 128], BF16))
        ps_o = en(nc.psum_tensor("ps_o", [64, 128], F32))

        p.dma(id_sb[:, :], ident)
        p.dma(bq_s[:, :], bq, allow_slow_non_contiguous=True)
        p.dma(bk_s[:, :], bk, allow_slow_non_contiguous=True)
        p.dma(bv_s[:, :], bv, allow_slow_non_contiguous=True)
        for (w_d, w_s) in ((wqT, wq_b), (wkT, wk_b), (wvT, wv_b)):
            src3 = w_d.rearrange("(c p) e -> p c e", p=128)
            _split_dma(p, stg, src3, 32, 4)
            p.v(lambda e, d=w_s: e.tensor_copy(d[:, :, :], stg[:, :, :]))
        xT3 = xT.rearrange("(c p) n -> p c n", p=128)
        for nt in range(16):
            n0 = nt * 512
            for dc in range(32):
                p.dma(xstg[:, :], xT3[:, dc, n0:n0 + 512])
                p.v(lambda e: e.tensor_copy(xbf[:, :], xstg[:, :]))
                st_ = (dc == 0)
                sp_ = (dc == 31)
                p.t(lambda e, dc=dc, st_=st_, sp_=sp_: e.matmul(
                    psq[:, :], wq_b[:, dc, :], xbf[:, :], start=st_, stop=sp_))
                p.tmm(lambda e, dc=dc, st_=st_, sp_=sp_: e.matmul(
                    psk[:, :], wk_b[:, dc, :], xbf[:, :], start=st_, stop=sp_))
                p.tmm(lambda e, dc=dc, st_=st_, sp_=sp_: e.matmul(
                    psv[:, :], wv_b[:, dc, :], xbf[:, :], start=st_, stop=sp_))
            p.v(lambda e, n0=n0: e.tensor_scalar_add(qT[:, n0:n0 + 512], psq[:, :], bq_s[:, 0:1]))
            p.v(lambda e, n0=n0: e.tensor_scalar_add(kT[:, n0:n0 + 512], psk[:, :], bk_s[:, 0:1]))
            p.v(lambda e, n0=n0: e.tensor_scalar_add(vT[:, n0:n0 + 512], psv[:, :], bv_s[:, 0:1]))
        for t in range(64):
            p.t(lambda e, t=t: e.transpose(ps_t[:, 0:64], vT[:, t * 128:(t + 1) * 128], id_sb[0:64, 0:64]))
            p.s(lambda e, t=t: e.copy(vn[:, t, :], ps_t[:, 0:64]))
        for qt in range(64):
            q0 = qt * 128
            for kt in range(16):
                k0 = kt * 512
                p.t(lambda e, q0=q0, k0=k0: e.matmul(
                    ps_s[:, :], qT[:, q0:q0 + 128], kT[:, k0:k0 + 512], start=True, stop=True))
                p.s(lambda e, k0=k0: e.copy(s_sb[:, k0:k0 + 512], ps_s[:, :]))
            p.v(lambda e: e.reduce_max(mx[:, 0:1], s_sb[:, :], axis=AXX))
            p.v(lambda e: e.tensor_scalar_mul(ngm[:, 0:1], mx[:, 0:1], -1.0))
            p.s(lambda e: e.activation(p_f[:, :], s_sb[:, :], AF.Exp, bias=ngm[:, 0:1]))
            p.v(lambda e: e.reduce_sum(mx[:, 0:1], p_f[:, :], axis=AXX))
            p.v(lambda e: e.reciprocal(rc[:, 0:1], mx[:, 0:1]))
            p.v(lambda e: e.tensor_scalar(p_b[:, :], p_f[:, :], rc[:, 0:1], INV_SQRT_512,
                                          op0=ALU.mult, op1=ALU.mult))
            for kt in range(64):
                p.t(lambda e, kt=kt: e.transpose(ps_t[:, :], p_b[:, kt * 128:(kt + 1) * 128], id_sb[:, :]))
                p.s(lambda e: e.copy(pT[:, :], ps_t[:, :]))
                p.t(lambda e, kt=kt: e.matmul(ps_o[:, :], vn[:, kt, :], pT[:, :],
                                              start=(kt == 0), stop=(kt == 63)))
            p.s(lambda e: e.copy(oT_t[:, :], ps_o[:, :]))
            p.dma(oT[:, q0:q0 + 128], oT_t[:, :])
        p.finish(nc)
    return nc


def _build_l0_proj():
    nc = bass.Bass("TRN2", target_bir_lowering=False)
    oTa = nc.dram_tensor("oTa", [512, 1024], BF16, kind="ExternalInput").ap()
    woT = nc.dram_tensor("woT", [512, 4096], F32, kind="ExternalInput").ap()
    wob = nc.dram_tensor("wob", [1, 4096], F32, kind="ExternalInput").ap()
    xr = nc.dram_tensor("xr", [1024, 4096], F32, kind="ExternalInput").ap()
    ones = nc.dram_tensor("ones", [1, 128], BF16, kind="ExternalInput").ap()
    y1 = nc.dram_tensor("y1", [1024, 4096], F32, kind="ExternalOutput").ap()
    y1b = nc.dram_tensor("y1b", [1024, 4096], BF16, kind="ExternalOutput").ap()

    p = Prog()
    with contextlib.ExitStack() as st:
        en = st.enter_context
        oT_sb = en(nc.sbuf_tensor("oT_sb", [128, 4, 1024], BF16))
        wo_f = en(nc.sbuf_tensor("wo_f", [128, 4, 4096], F32))
        wo_b = en(nc.sbuf_tensor("wo_b", [128, 4, 4096], BF16))
        wob_f = en(nc.sbuf_tensor("wob_f", [1, 4096], F32))
        wob_b = en(nc.sbuf_tensor("wob_b", [1, 4096], BF16))
        on_sb = en(nc.sbuf_tensor("on_sb", [1, 128], BF16))
        xr_t = en(nc.sbuf_tensor("xr_t", [128, 4096], F32))
        z = en(nc.sbuf_tensor("z", [128, 4096], F32))
        zc = en(nc.sbuf_tensor("zc", [128, 4096], F32))
        yf = en(nc.sbuf_tensor("yf", [128, 4096], F32))
        yb = en(nc.sbuf_tensor("yb", [128, 4096], BF16))
        sm = en(nc.sbuf_tensor("sm", [128, 1], F32))
        ngm = en(nc.sbuf_tensor("ngm", [128, 1], F32))
        stdv = en(nc.sbuf_tensor("stdv", [128, 1], F32))
        rstd = en(nc.sbuf_tensor("rstd", [128, 1], F32))
        ps = en(nc.psum_tensor("ps", [128, 512], F32))

        p.dma(on_sb[:, :], ones)
        p.dma(wob_f[:, :], wob)
        p.v(lambda e: e.tensor_copy(wob_b[:, :], wob_f[:, :]))
        oTa3 = oTa.rearrange("(c p) m -> p c m", p=128)
        p.dma(oT_sb[:, :, :], oTa3)
        woT3 = woT.rearrange("(c p) n -> p c n", p=128)
        p.dma(wo_f[:, :, :], woT3)
        p.v(lambda e: e.tensor_copy(wo_b[:, :, :], wo_f[:, :, :]))
        for mt in range(8):
            m0 = mt * 128
            p.dma(xr_t[:, :], xr[m0:m0 + 128, :])
            for ntl in range(8):
                n0 = ntl * 512
                for dc in range(4):
                    em = p.t if dc == 0 else p.tmm
                    em(lambda e, dc=dc, m0=m0, n0=n0: e.matmul(
                        ps[:, :], oT_sb[:, dc, m0:m0 + 128], wo_b[:, dc, n0:n0 + 512],
                        start=(dc == 0), stop=False))
                p.tmm(lambda e, n0=n0: e.matmul(ps[:, :], on_sb[:, :], wob_b[:, n0:n0 + 512],
                                                start=False, stop=True))
                p.v(lambda e, n0=n0: e.tensor_add(z[:, n0:n0 + 512], ps[:, :], xr_t[:, n0:n0 + 512]))
            _layernorm_tail(p, z[:, :], zc, (sm, ngm, stdv, rstd), yf[:, :], yb[:, :])
            p.dma(y1[m0:m0 + 128, :], yf[:, :])
            p.dma(y1b[m0:m0 + 128, :], yb[:, :])
        p.finish(nc)
    return nc


def _dequant_group(p, qw_sb, unp_i, unp_f, qz_sb, z_i, z_f, s_f, ones_sb,
                   ps_z, ps_sc, W_dst, wtile, subw, nsub):
    unp3 = unp_i.rearrange("p (w i) -> p w i", i=8)
    z3 = z_i.rearrange("p (w i) -> p w i", i=8)
    for i in range(8):
        sh = 4 * ARO[i]
        p.v(lambda e, i=i, sh=sh: e.tensor_scalar(
            unp3[:, :, i], qw_sb[:, :], sh, 15, op0=ALU.logical_shift_right, op1=ALU.bitwise_and))
    p.v(lambda e: e.tensor_copy(unp_f[:, :], unp_i[:, :]))
    for i in range(8):
        sh = 4 * ARO[i]
        p.v(lambda e, i=i, sh=sh: e.tensor_scalar(
            z3[:, :, i], qz_sb[:, :], sh, 15, op0=ALU.logical_shift_right, op1=ALU.bitwise_and))
    p.v(lambda e: e.tensor_copy(z_f[:, :], z_i[:, :]))
    for sub in range(nsub):
        c0 = sub * subw
        p.t(lambda e, c0=c0: e.matmul(ps_z[:, :], ones_sb[:, :], z_f[:, c0:c0 + subw],
                                      start=True, stop=True))
        p.tmm(lambda e, c0=c0: e.matmul(ps_sc[:, :], ones_sb[:, :], s_f[:, c0:c0 + subw],
                                        start=True, stop=True))
        p.v(lambda e, c0=c0: e.tensor_sub(unp_f[:, c0:c0 + subw], unp_f[:, c0:c0 + subw], ps_z[:, :]))
        p.v(lambda e, c0=c0: e.tensor_tensor(W_dst[:, c0:c0 + subw], unp_f[:, c0:c0 + subw],
                                             ps_sc[:, :], op=ALU.mult))


def _build_ffn_gu():
    nc = bass.Bass("TRN2", target_bir_lowering=False)
    y1T = nc.dram_tensor("y1T", [4096, 8192], BF16, kind="ExternalInput").ap()
    gqw = nc.dram_tensor("gqw", [4096, 172], I32, kind="ExternalInput").ap()
    gsc = nc.dram_tensor("gsc", [32, 1376], F32, kind="ExternalInput").ap()
    gqz = nc.dram_tensor("gqz", [32, 172], I32, kind="ExternalInput").ap()
    uqw = nc.dram_tensor("uqw", [4096, 172], I32, kind="ExternalInput").ap()
    usc = nc.dram_tensor("usc", [32, 1376], F32, kind="ExternalInput").ap()
    uqz = nc.dram_tensor("uqz", [32, 172], I32, kind="ExternalInput").ap()
    onesf = nc.dram_tensor("onesf", [1, 128], F32, kind="ExternalInput").ap()
    h = nc.dram_tensor("h", [8192, 1376], BF16, kind="ExternalOutput").ap()

    p = Prog()
    with contextlib.ExitStack() as st:
        en = st.enter_context
        Wg = en(nc.sbuf_tensor("Wg", [128, 32, 688], BF16))
        Wu = en(nc.sbuf_tensor("Wu", [128, 32, 688], BF16))
        xt = en(nc.sbuf_tensor("xt", [128, 32, 128], BF16))
        qw_sb = en(nc.sbuf_tensor("qw_sb", [128, 86], I32))
        unp_i = en(nc.sbuf_tensor("unp_i", [128, 688], I32))
        unp_f = en(nc.sbuf_tensor("unp_f", [128, 688], F32))
        qz_sb = en(nc.sbuf_tensor("qz_sb", [1, 86], I32))
        z_i = en(nc.sbuf_tensor("z_i", [1, 688], I32))
        z_f = en(nc.sbuf_tensor("z_f", [1, 688], F32))
        s_f = en(nc.sbuf_tensor("s_f", [1, 688], F32))
        on_sb = en(nc.sbuf_tensor("on_sb", [1, 128], F32))
        sig = en(nc.sbuf_tensor("sig", [128, 344], F32))
        hf = en(nc.sbuf_tensor("hf", [128, 344], F32))
        hb = en(nc.sbuf_tensor("hb", [128, 344], BF16))
        ps_z = en(nc.psum_tensor("ps_z", [128, 344], F32))
        ps_sc = en(nc.psum_tensor("ps_sc", [128, 344], F32))
        ps_g = en(nc.psum_tensor("ps_g", [128, 344], F32))
        ps_u = en(nc.psum_tensor("ps_u", [128, 344], F32))

        p.dma(on_sb[:, :], onesf)
        y1T3 = y1T.rearrange("(c p) n -> p c n", p=128)
        for h2 in range(2):
            w0 = 86 * h2
            c0h = 688 * h2
            for (qwd, scd, qzd, Wdst) in ((gqw, gsc, gqz, Wg), (uqw, usc, uqz, Wu)):
                for g in range(32):
                    p.dma(qw_sb[:, :], qwd[g * 128:(g + 1) * 128, w0:w0 + 86])
                    p.dma(qz_sb[:, :], qzd[g:g + 1, w0:w0 + 86])
                    p.dma(s_f[:, :], scd[g:g + 1, c0h:c0h + 688])
                    Wrow = Wdst[:, g, :]
                    _dequant_group(p, qw_sb, unp_i, unp_f, qz_sb, z_i, z_f, s_f,
                                   on_sb, ps_z, ps_sc, Wrow, None, 344, 2)
            for mt in range(64):
                m0 = mt * 128
                _split_dma(p, xt, y1T3[:, :, m0:m0 + 128], 32, 4)
                for sub in range(2):
                    c0 = sub * 344
                    for dc in range(32):
                        em = p.t if (dc == 0 and sub == 0) else p.tmm
                        em(lambda e, dc=dc, c0=c0: e.matmul(
                            ps_g[:, :], xt[:, dc, :], Wg[:, dc, c0:c0 + 344],
                            start=(dc == 0), stop=(dc == 31)))
                    for dc in range(32):
                        p.tmm(lambda e, dc=dc, c0=c0: e.matmul(
                            ps_u[:, :], xt[:, dc, :], Wu[:, dc, c0:c0 + 344],
                            start=(dc == 0), stop=(dc == 31)))
                    p.s(lambda e: e.activation(sig[:, :], ps_g[:, :], AF.Sigmoid))
                    p.v(lambda e: e.tensor_mul(hf[:, :], sig[:, :], ps_g[:, :]))
                    p.v(lambda e: e.tensor_mul(hb[:, :], hf[:, :], ps_u[:, :]))
                    p.dma(h[m0:m0 + 128, c0h + c0:c0h + c0 + 344], hb[:, :])
        p.finish(nc)
    return nc


def _build_ffn_down():
    nc = bass.Bass("TRN2", target_bir_lowering=False)
    hT = nc.dram_tensor("hT", [11008, 8192], BF16, kind="ExternalInput").ap()
    dqw = nc.dram_tensor("dqw", [11008, 64], I32, kind="ExternalInput").ap()
    dsc = nc.dram_tensor("dsc", [86, 512], F32, kind="ExternalInput").ap()
    dqz = nc.dram_tensor("dqz", [86, 64], I32, kind="ExternalInput").ap()
    onesf = nc.dram_tensor("onesf", [1, 128], F32, kind="ExternalInput").ap()
    fo = nc.dram_tensor("fo", [8192, 512], F32, kind="ExternalOutput").ap()

    p = Prog()
    with contextlib.ExitStack() as st:
        en = st.enter_context
        Wd = en(nc.sbuf_tensor("Wd", [128, 86, 512], BF16))
        ht = en(nc.sbuf_tensor("ht", [128, 86, 128], BF16))
        qw_sb = en(nc.sbuf_tensor("qw_sb", [128, 64], I32))
        unp_i = en(nc.sbuf_tensor("unp_i", [128, 512], I32))
        unp_f = en(nc.sbuf_tensor("unp_f", [128, 512], F32))
        qz_sb = en(nc.sbuf_tensor("qz_sb", [1, 64], I32))
        z_i = en(nc.sbuf_tensor("z_i", [1, 512], I32))
        z_f = en(nc.sbuf_tensor("z_f", [1, 512], F32))
        s_f = en(nc.sbuf_tensor("s_f", [1, 512], F32))
        on_sb = en(nc.sbuf_tensor("on_sb", [1, 128], F32))
        fo_t = en(nc.sbuf_tensor("fo_t", [128, 512], F32))
        ps_z = en(nc.psum_tensor("ps_z", [128, 512], F32))
        ps_sc = en(nc.psum_tensor("ps_sc", [128, 512], F32))
        ps = en(nc.psum_tensor("ps", [128, 512], F32))

        p.dma(on_sb[:, :], onesf)
        for g in range(86):
            p.dma(qw_sb[:, :], dqw[g * 128:(g + 1) * 128, :])
            p.dma(qz_sb[:, :], dqz[g:g + 1, :])
            p.dma(s_f[:, :], dsc[g:g + 1, :])
            _dequant_group(p, qw_sb, unp_i, unp_f, qz_sb, z_i, z_f, s_f,
                           on_sb, ps_z, ps_sc, Wd[:, g, :], None, 512, 1)
        hT3 = hT.rearrange("(c p) n -> p c n", p=128)
        for mt in range(64):
            m0 = mt * 128
            _split_dma(p, ht, hT3[:, :, m0:m0 + 128], 86, 9)
            for dc in range(86):
                em = p.t if dc == 0 else p.tmm
                em(lambda e, dc=dc: e.matmul(ps[:, :], ht[:, dc, :], Wd[:, dc, :],
                                             start=(dc == 0), stop=(dc == 85)))
            p.s(lambda e: e.copy(fo_t[:, :], ps[:, :]))
            p.dma(fo[m0:m0 + 128, :], fo_t[:, :])
        p.finish(nc)
    return nc


def _build_res_ln():
    nc = bass.Bass("TRN2", target_bir_lowering=False)
    a = nc.dram_tensor("a", [1024, 4096], F32, kind="ExternalInput").ap()
    b = nc.dram_tensor("b", [1024, 4096], F32, kind="ExternalInput").ap()
    y = nc.dram_tensor("y", [1024, 4096], F32, kind="ExternalOutput").ap()
    yb = nc.dram_tensor("yb", [1024, 4096], BF16, kind="ExternalOutput").ap()

    p = Prog()
    with contextlib.ExitStack() as st:
        en = st.enter_context
        at = en(nc.sbuf_tensor("at", [128, 4096], F32))
        bt = en(nc.sbuf_tensor("bt", [128, 4096], F32))
        z = en(nc.sbuf_tensor("z", [128, 4096], F32))
        zc = en(nc.sbuf_tensor("zc", [128, 4096], F32))
        yf = en(nc.sbuf_tensor("yf", [128, 4096], F32))
        ybt = en(nc.sbuf_tensor("ybt", [128, 4096], BF16))
        sm = en(nc.sbuf_tensor("sm", [128, 1], F32))
        ngm = en(nc.sbuf_tensor("ngm", [128, 1], F32))
        stdv = en(nc.sbuf_tensor("stdv", [128, 1], F32))
        rstd = en(nc.sbuf_tensor("rstd", [128, 1], F32))
        for mt in range(8):
            m0 = mt * 128
            p.dma(at[:, :], a[m0:m0 + 128, :])
            p.dma(bt[:, :], b[m0:m0 + 128, :])
            p.v(lambda e: e.tensor_add(z[:, :], at[:, :], bt[:, :]))
            _layernorm_tail(p, z[:, :], zc, (sm, ngm, stdv, rstd), yf[:, :], ybt[:, :])
            p.dma(y[m0:m0 + 128, :], yf[:, :])
            p.dma(yb[m0:m0 + 128, :], ybt[:, :])
        p.finish(nc)
    return nc


def _build_l1_attn():
    nc = bass.Bass("TRN2", target_bir_lowering=False)
    y2T = nc.dram_tensor("y2T", [4096, 8192], BF16, kind="ExternalInput").ap()
    wqTc = nc.dram_tensor("wqTc", [4096, 512], F32, kind="ExternalInput").ap()
    wkTc = nc.dram_tensor("wkTc", [4096, 128], F32, kind="ExternalInput").ap()
    wvTc = nc.dram_tensor("wvTc", [4096, 128], F32, kind="ExternalInput").ap()
    ident = nc.dram_tensor("ident", [128, 128], BF16, kind="ExternalInput").ap()
    tri = nc.dram_tensor("tri", [128, 128], F32, kind="ExternalInput").ap()
    upp = nc.dram_tensor("upp", [128, 128], F32, kind="ExternalInput").ap()
    oTc = nc.dram_tensor("oTc", [512, 8192], BF16, kind="ExternalOutput").ap()

    p = Prog()
    with contextlib.ExitStack() as st:
        en = st.enter_context
        wq_b = en(nc.sbuf_tensor("wq_b", [128, 32, 512], BF16))
        wk_b = en(nc.sbuf_tensor("wk_b", [128, 32, 128], BF16))
        wv_b = en(nc.sbuf_tensor("wv_b", [128, 32, 128], BF16))
        stg = en(nc.sbuf_tensor("stg", [128, 512], F32))
        ystg = en(nc.sbuf_tensor("ystg", [128, 512], BF16))
        id_sb = en(nc.sbuf_tensor("id_sb", [128, 128], BF16))
        tri_s = en(nc.sbuf_tensor("tri_s", [128, 128], F32))
        upp_s = en(nc.sbuf_tensor("upp_s", [128, 128], F32))
        qT = en(nc.sbuf_tensor("qT", [128, 4, 8192], BF16))
        kT = en(nc.sbuf_tensor("kT", [128, 8192], BF16))
        vT = en(nc.sbuf_tensor("vT", [128, 8192], BF16))
        vn = en(nc.sbuf_tensor("vn", [128, 64, 128], BF16))
        s_sb = en(nc.sbuf_tensor("s_sb", [128, 640], F32))
        p_f = en(nc.sbuf_tensor("p_f", [128, 640], F32))
        p_b = en(nc.sbuf_tensor("p_b", [128, 640], BF16))
        pT = en(nc.sbuf_tensor("pT", [128, 128], BF16))
        oT_t = en(nc.sbuf_tensor("oT_t", [128, 128], BF16))
        mx = en(nc.sbuf_tensor("mx", [128, 1], F32))
        ngm = en(nc.sbuf_tensor("ngm", [128, 1], F32))
        rc = en(nc.sbuf_tensor("rc", [128, 1], F32))
        psq = [en(nc.psum_tensor("psq%d" % i, [128, 512], F32)) for i in range(4)]
        psk = en(nc.psum_tensor("psk", [128, 512], F32))
        psv = en(nc.psum_tensor("psv", [128, 512], F32))
        ps_t = en(nc.psum_tensor("ps_t", [128, 128], BF16))

        ps_s = psq[0]
        ps_o = psq[1]

        p.dma(id_sb[:, :], ident)
        p.dma(tri_s[:, :], tri)
        p.dma(upp_s[:, :], upp)
        for dc in range(32):
            p.dma(stg[:, :], wqTc[dc * 128:(dc + 1) * 128, :])
            p.v(lambda e, dc=dc: e.tensor_copy(wq_b[:, dc, :], stg[:, :]))
            p.dma(stg[:, 0:128], wkTc[dc * 128:(dc + 1) * 128, :])
            p.v(lambda e, dc=dc: e.tensor_copy(wk_b[:, dc, :], stg[:, 0:128]))
            p.dma(stg[:, 0:128], wvTc[dc * 128:(dc + 1) * 128, :])
            p.v(lambda e, dc=dc: e.tensor_copy(wv_b[:, dc, :], stg[:, 0:128]))
        y2T3 = y2T.rearrange("(c p) n -> p c n", p=128)
        for nt in range(16):
            n0 = nt * 512
            for dc in range(32):
                p.dma(ystg[:, :], y2T3[:, dc, n0:n0 + 512])
                st_ = (dc == 0)
                sp_ = (dc == 31)
                for hh in range(4):
                    em = p.t if hh == 0 else p.tmm
                    em(lambda e, dc=dc, hh=hh, st_=st_, sp_=sp_: e.matmul(
                        psq[hh][:, :], wq_b[:, dc, hh * 128:(hh + 1) * 128], ystg[:, :],
                        start=st_, stop=sp_))
                p.tmm(lambda e, dc=dc, st_=st_, sp_=sp_: e.matmul(
                    psk[:, :], wk_b[:, dc, :], ystg[:, :], start=st_, stop=sp_))
                p.tmm(lambda e, dc=dc, st_=st_, sp_=sp_: e.matmul(
                    psv[:, :], wv_b[:, dc, :], ystg[:, :], start=st_, stop=sp_))
            for hh in range(4):
                p.s(lambda e, hh=hh, n0=n0: e.copy(qT[:, hh, n0:n0 + 512], psq[hh][:, :]))
            p.s(lambda e, n0=n0: e.copy(kT[:, n0:n0 + 512], psk[:, :]))
            p.s(lambda e, n0=n0: e.copy(vT[:, n0:n0 + 512], psv[:, :]))
        for t in range(64):
            p.t(lambda e, t=t: e.transpose(ps_t[:, :], vT[:, t * 128:(t + 1) * 128], id_sb[:, :]))
            p.s(lambda e, t=t: e.copy(vn[:, t, :], ps_t[:, :]))
        for qb in range(64):
            q0 = qb * 128
            t0 = max(0, qb - 4)
            ntk = qb - t0 + 1
            w = ntk * 128
            for hh in range(4):
                for j in range(ntk):
                    kt = t0 + j
                    p.t(lambda e, hh=hh, q0=q0, kt=kt: e.matmul(
                        ps_s[:, 0:128], qT[:, hh, q0:q0 + 128], kT[:, kt * 128:(kt + 1) * 128],
                        start=True, stop=True, skip_group_check=True))
                    if kt == qb:
                        p.v(lambda e, j=j: e.scalar_tensor_tensor(
                            s_sb[:, j * 128:(j + 1) * 128], ps_s[:, 0:128], INV_SQRT_128,
                            tri_s[:, :], op0=ALU.mult, op1=ALU.add))
                    elif kt == qb - 4:
                        p.v(lambda e, j=j: e.scalar_tensor_tensor(
                            s_sb[:, j * 128:(j + 1) * 128], ps_s[:, 0:128], INV_SQRT_128,
                            upp_s[:, :], op0=ALU.mult, op1=ALU.add))
                    else:
                        p.v(lambda e, j=j: e.tensor_scalar_mul(
                            s_sb[:, j * 128:(j + 1) * 128], ps_s[:, 0:128], INV_SQRT_128))
                p.v(lambda e, w=w: e.reduce_max(mx[:, 0:1], s_sb[:, 0:w], axis=AXX))
                p.v(lambda e: e.tensor_scalar_mul(ngm[:, 0:1], mx[:, 0:1], -1.0))
                p.s(lambda e, w=w: e.activation(p_f[:, 0:w], s_sb[:, 0:w], AF.Exp, bias=ngm[:, 0:1]))
                p.v(lambda e, w=w: e.reduce_sum(mx[:, 0:1], p_f[:, 0:w], axis=AXX))
                p.v(lambda e: e.reciprocal(rc[:, 0:1], mx[:, 0:1]))
                p.v(lambda e, w=w: e.tensor_scalar_mul(p_b[:, 0:w], p_f[:, 0:w], rc[:, 0:1]))
                for j in range(ntk):
                    kt = t0 + j
                    p.t(lambda e, j=j: e.transpose(ps_t[:, :], p_b[:, j * 128:(j + 1) * 128], id_sb[:, :]))
                    p.s(lambda e: e.copy(pT[:, :], ps_t[:, :]))
                    p.t(lambda e, kt=kt, j=j, ntk=ntk: e.matmul(
                        ps_o[:, 0:128], vn[:, kt, :], pT[:, :],
                        start=(j == 0), stop=(j == ntk - 1), skip_group_check=True))
                p.s(lambda e: e.copy(oT_t[:, :], ps_o[:, 0:128]))
                p.dma(oTc[hh * 128:(hh + 1) * 128, q0:q0 + 128], oT_t[:, :])
        p.finish(nc)
    return nc


def _build_proj_res_ln(rhs_name, rhs_is_sign):
    nc = bass.Bass("TRN2", target_bir_lowering=False)
    lT = nc.dram_tensor("lT", [4096, 1024], BF16, kind="ExternalInput").ap()
    wf = nc.dram_tensor("wf", [4096, 4096], F32, kind="ExternalInput").ap()
    xres = nc.dram_tensor("xres", [1024, 4096], F32, kind="ExternalInput").ap()
    if rhs_is_sign:
        scl = nc.dram_tensor("scl", [1, 1], F32, kind="ExternalInput").ap()
        onesf = nc.dram_tensor("onesf", [1, 128], F32, kind="ExternalInput").ap()
    y = nc.dram_tensor("y", [1024, 4096], F32, kind="ExternalOutput").ap()
    yb = nc.dram_tensor("yb", [1024, 4096], BF16, kind="ExternalOutput").ap()

    p = Prog()
    with contextlib.ExitStack() as st:
        en = st.enter_context
        w_b = en(nc.sbuf_tensor("w_b", [128, 32, 512], BF16))
        stg = en(nc.sbuf_tensor("stg", [128, 512], F32))
        l_mt = en(nc.sbuf_tensor("l_mt", [128, 32, 128], BF16))
        z = en(nc.sbuf_tensor("z", [128, 4, 4096], F32))
        xr_t = en(nc.sbuf_tensor("xr_t", [128, 4096], F32))
        zc = en(nc.sbuf_tensor("zc", [128, 4096], F32))
        yf = en(nc.sbuf_tensor("yf", [128, 4096], F32))
        ybt = en(nc.sbuf_tensor("ybt", [128, 4096], BF16))
        sm = en(nc.sbuf_tensor("sm", [128, 1], F32))
        ngm = en(nc.sbuf_tensor("ngm", [128, 1], F32))
        stdv = en(nc.sbuf_tensor("stdv", [128, 1], F32))
        rstd = en(nc.sbuf_tensor("rstd", [128, 1], F32))
        ps = en(nc.psum_tensor("ps", [128, 512], F32))
        if rhs_is_sign:
            scl_s = en(nc.sbuf_tensor("scl_s", [1, 1], F32))
            sp1 = en(nc.sbuf_tensor("sp1", [1, 1], F32))
            sp2 = en(nc.sbuf_tensor("sp2", [1, 1], F32))
            spb = en(nc.sbuf_tensor("spb", [128, 1], F32))
            on_sb = en(nc.sbuf_tensor("on_sb", [1, 128], F32))
            ps_sp = en(nc.psum_tensor("ps_sp", [128, 1], F32))
            p.dma(on_sb[:, :], onesf)
            p.dma(scl_s[:, :], scl, allow_slow_non_contiguous=True)
            p.s(lambda e: e.activation(sp1[:, :], scl_s[:, :], AF.Abs))
            p.v(lambda e: e.tensor_scalar_mul(sp1[:, :], sp1[:, :], -1.0))
            p.s(lambda e: e.activation(sp1[:, :], sp1[:, :], AF.Exp))
            p.v(lambda e: e.tensor_scalar_add(sp1[:, :], sp1[:, :], 1.0))
            p.s(lambda e: e.activation(sp1[:, :], sp1[:, :], AF.Ln))
            p.v(lambda e: e.tensor_scalar_max(sp2[:, :], scl_s[:, :], 0.0))
            p.v(lambda e: e.tensor_add(sp1[:, :], sp1[:, :], sp2[:, :]))
            p.t(lambda e: e.matmul(ps_sp[:, :], on_sb[:, :], sp1[:, :], start=True, stop=True))
            p.v(lambda e: e.tensor_copy(spb[:, :], ps_sp[:, :]))
        lT3 = lT.rearrange("(c p) n -> p c n", p=128)
        for half in range(2):
            for ntl in range(8):
                n0 = ntl * 512
                for dc in range(32):
                    p.dma(stg[:, :], wf[dc * 128:(dc + 1) * 128, n0:n0 + 512])
                    p.v(lambda e, dc=dc: e.tensor_copy(w_b[:, dc, :], stg[:, :]))
                for mt4 in range(4):
                    mtg = half * 4 + mt4
                    m0 = mtg * 128
                    _split_dma(p, l_mt, lT3[:, :, m0:m0 + 128], 32, 4)
                    for dc in range(32):
                        em = p.t if dc == 0 else p.tmm
                        em(lambda e, dc=dc: e.matmul(ps[:, :], l_mt[:, dc, :], w_b[:, dc, :],
                                                     start=(dc == 0), stop=(dc == 31)))
                    if rhs_is_sign:
                        p.v(lambda e, mt4=mt4, n0=n0: e.tensor_scalar_mul(
                            z[:, mt4, n0:n0 + 512], ps[:, :], spb[:, 0:1]))
                    else:
                        p.v(lambda e, mt4=mt4, n0=n0: e.tensor_copy(z[:, mt4, n0:n0 + 512], ps[:, :]))
            for mt4 in range(4):
                mtg = half * 4 + mt4
                m0 = mtg * 128
                p.dma(xr_t[:, :], xres[m0:m0 + 128, :])
                p.v(lambda e, mt4=mt4: e.tensor_add(z[:, mt4, :], z[:, mt4, :], xr_t[:, :]))
                _layernorm_tail(p, z[:, mt4, :], zc, (sm, ngm, stdv, rstd), yf[:, :], ybt[:, :])
                p.dma(y[m0:m0 + 128, :], yf[:, :])
                p.dma(yb[m0:m0 + 128, :], ybt[:, :])
        p.finish(nc)
    return nc


def _run(nc, in_maps):
    res = run_bass_kernel_spmd(nc, in_maps, core_ids=list(range(8)))
    return res.results


def kernel(**inputs):
    f32 = np.float32
    x = np.ascontiguousarray(inputs["L0_attn_x"][0])
    xT = np.ascontiguousarray(x.T)

    ident = np.eye(128, dtype=BFD)
    ones_b = np.ones((1, 128), dtype=BFD)
    ones_f = np.ones((1, 128), dtype=f32)
    ii = np.arange(128)[:, None]
    jj = np.arange(128)[None, :]
    tri = np.where(jj <= ii, 0.0, NEG).astype(f32)
    upp = np.where(jj > ii, 0.0, NEG).astype(f32)

    wqk = inputs["L0_attn_Wqk_w"].reshape(8, 64, 2, 4096)
    bqk = inputs["L0_attn_Wqk_b"].reshape(8, 64, 2)
    wv = inputs["L0_attn_Wv_w"].reshape(8, 64, 4096)
    bv = inputs["L0_attn_Wv_b"].reshape(8, 64)
    nc1 = _build_l0_attn()
    maps = []
    for h in range(8):
        maps.append({
            "xT": xT,
            "wqT": np.ascontiguousarray(wqk[h, :, 0, :].T),
            "wkT": np.ascontiguousarray(wqk[h, :, 1, :].T),
            "wvT": np.ascontiguousarray(wv[h].T),
            "bq": np.ascontiguousarray(bqk[h, :, 0].reshape(64, 1)),
            "bk": np.ascontiguousarray(bqk[h, :, 1].reshape(64, 1)),
            "bv": np.ascontiguousarray(bv[h].reshape(64, 1)),
            "ident": ident,
        })
    r1 = _run(nc1, maps)
    oTa = np.concatenate([r1[h]["oT"] for h in range(8)], axis=0)

    woT = np.ascontiguousarray(inputs["L0_attn_Wo_w"].T)
    wob = np.ascontiguousarray(inputs["L0_attn_Wo_b"].reshape(1, 4096))
    nc2 = _build_l0_proj()
    maps = [{"oTa": np.ascontiguousarray(oTa[:, c * 1024:(c + 1) * 1024]),
             "woT": woT, "wob": wob, "ones": ones_b,
             "xr": np.ascontiguousarray(x[c * 1024:(c + 1) * 1024])} for c in range(8)]
    r2 = _run(nc2, maps)
    y1 = np.concatenate([r2[c]["y1"] for c in range(8)], axis=0)
    y1b = np.concatenate([r2[c]["y1b"] for c in range(8)], axis=0)
    y1T = np.ascontiguousarray(y1b.T)

    nc3 = _build_ffn_gu()
    maps = []
    for c in range(8):
        maps.append({
            "y1T": y1T,
            "gqw": np.ascontiguousarray(inputs["L0_ffn_gate_qweight"][:, c * 172:(c + 1) * 172]),
            "gsc": np.ascontiguousarray(inputs["L0_ffn_gate_scales"][:, c * 1376:(c + 1) * 1376]),
            "gqz": np.ascontiguousarray(inputs["L0_ffn_gate_qzeros"][:, c * 172:(c + 1) * 172]),
            "uqw": np.ascontiguousarray(inputs["L0_ffn_up_qweight"][:, c * 172:(c + 1) * 172]),
            "usc": np.ascontiguousarray(inputs["L0_ffn_up_scales"][:, c * 1376:(c + 1) * 1376]),
            "uqz": np.ascontiguousarray(inputs["L0_ffn_up_qzeros"][:, c * 172:(c + 1) * 172]),
            "onesf": ones_f,
        })
    r3 = _run(nc3, maps)
    hfull = np.concatenate([r3[c]["h"] for c in range(8)], axis=1)
    hT = np.ascontiguousarray(hfull.T)

    nc4 = _build_ffn_down()
    maps = []
    for c in range(8):
        maps.append({
            "hT": hT,
            "dqw": np.ascontiguousarray(inputs["L0_ffn_down_qweight"][:, c * 64:(c + 1) * 64]),
            "dsc": np.ascontiguousarray(inputs["L0_ffn_down_scales"][:, c * 512:(c + 1) * 512]),
            "dqz": np.ascontiguousarray(inputs["L0_ffn_down_qzeros"][:, c * 64:(c + 1) * 64]),
            "onesf": ones_f,
        })
    r4 = _run(nc4, maps)
    ffo = np.concatenate([r4[c]["fo"] for c in range(8)], axis=1)

    nc5 = _build_res_ln()
    maps = [{"a": np.ascontiguousarray(y1[c * 1024:(c + 1) * 1024]),
             "b": np.ascontiguousarray(ffo[c * 1024:(c + 1) * 1024])} for c in range(8)]
    r5 = _run(nc5, maps)
    y2 = np.concatenate([r5[c]["y"] for c in range(8)], axis=0)
    y2b = np.concatenate([r5[c]["yb"] for c in range(8)], axis=0)
    y2T = np.ascontiguousarray(y2b.T)

    wqT_full = np.ascontiguousarray(inputs["L1_attn_wq"].T)
    wkT_full = np.ascontiguousarray(inputs["L1_attn_wk"].T)
    wvT_full = np.ascontiguousarray(inputs["L1_attn_wv"].T)
    nc6 = _build_l1_attn()
    maps = []
    for c in range(8):
        maps.append({
            "y2T": y2T,
            "wqTc": np.ascontiguousarray(wqT_full[:, c * 512:(c + 1) * 512]),
            "wkTc": np.ascontiguousarray(wkT_full[:, c * 128:(c + 1) * 128]),
            "wvTc": np.ascontiguousarray(wvT_full[:, c * 128:(c + 1) * 128]),
            "ident": ident, "tri": tri, "upp": upp,
        })
    r6 = _run(nc6, maps)
    oT1 = np.concatenate([r6[c]["oTc"] for c in range(8)], axis=0)

    woT1 = np.ascontiguousarray(inputs["L1_attn_wo"].T)
    nc7 = _build_proj_res_ln("wo", False)
    maps = [{"lT": np.ascontiguousarray(oT1[:, c * 1024:(c + 1) * 1024]), "wf": woT1,
             "xres": np.ascontiguousarray(y2[c * 1024:(c + 1) * 1024])} for c in range(8)]
    r7 = _run(nc7, maps)
    y3 = np.concatenate([r7[c]["y"] for c in range(8)], axis=0)
    y3b = np.concatenate([r7[c]["yb"] for c in range(8)], axis=0)
    y3T = np.ascontiguousarray(y3b.T)

    sgn = np.ascontiguousarray(inputs["L1_ffn_sign"])
    scl = np.asarray(inputs["L1_ffn_scale"], dtype=f32).reshape(1, 1)
    nc8 = _build_proj_res_ln("sign", True)
    maps = [{"lT": np.ascontiguousarray(y3T[:, c * 1024:(c + 1) * 1024]),
             "wf": sgn, "scl": scl, "onesf": ones_f,
             "xres": np.ascontiguousarray(y3[c * 1024:(c + 1) * 1024])} for c in range(8)]
    r8 = _run(nc8, maps)
    out = np.concatenate([r8[c]["y"] for c in range(8)], axis=0)
    return out.reshape(1, 8192, 4096).astype(np.float32)
